# version 05 (final)
# baseline (speedup 1.0000x reference)
from contextlib import ExitStack

import numpy as np
import ml_dtypes

import concourse.bacc as bacc
import concourse.bass as bass
import concourse.mybir as mybir
from concourse.bass_utils import run_bass_kernel_spmd
from concourse.library_config import (
    mlp as LIB_MLP,
    sparse_gather as LIB_SG,
    attn as LIB_ATTN,
    standard as LIB_STD,
)

F32 = mybir.dt.float32
BF16 = mybir.dt.bfloat16
I16 = mybir.dt.int16
U32 = mybir.dt.uint32
OP = mybir.AluOpType
AF = mybir.ActivationFunctionType
AX = mybir.AxisListType

NCORE = 8
D = 1024
S = 1024
FA = 2048
DFF = 4096
E = 8
KSEL = 2048
TOK = 32768
TPC = 4096
CH = 512
NCH = TPC // CH
DC = D // 128
SC = S // 128
FB = FA // 128
EFB = DFF // 128
LN_EPS = 1e-5
SLAB_ROWS = 4224
GCAP_COLS = 136

_NV1 = TOK - 1
_OMQ = int(round((KSEL - 0.5) / _NV1 * (1 << 32)))
assert (_OMQ * _NV1) >> 32 == KSEL - 1
KTH_Q = 1.0 - _OMQ / (1 << 32)

DEBUG_OUT = ()


def _patch_act_tables():
    import concourse.hw_specs as hw_specs
    if getattr(hw_specs, "_mk_patched", False):
        return
    orig = hw_specs.get_activation_tables

    def patched(arch):
        tabs = orig(arch)
        out = {}
        for i, (name, funcs) in enumerate(tabs.items()):
            out[name] = funcs if i == 6 else type(funcs)()
        return out

    hw_specs.get_activation_tables = patched
    hw_specs._mk_patched = True


def build_program(debug=(), stop_after="full", nchunks=NCH, skip=(), dump=None, dtap=None):
    _patch_act_tables()
    nc = bacc.Bacc("TRN2", detect_race_conditions=False)

    xT = nc.dram_tensor("xT", [D, TPC], F32, kind="ExternalInput")
    w_in_t = nc.dram_tensor("w_in_t", [D, S], F32, kind="ExternalInput")
    w_out_t = nc.dram_tensor("w_out_t", [S, D], F32, kind="ExternalInput")
    w1a_t = nc.dram_tensor("w1a_t", [D, FA], F32, kind="ExternalInput")
    w2a_t = nc.dram_tensor("w2a_t", [FA, D], F32, kind="ExternalInput")
    rwt = nc.dram_tensor("rwt", [D, E], F32, kind="ExternalInput")
    rbr = nc.dram_tensor("rbr", [128, E], F32, kind="ExternalInput")
    we1 = nc.dram_tensor("we1", [D, DFF], F32, kind="ExternalInput")
    we2 = nc.dram_tensor("we2", [DFF, D], F32, kind="ExternalInput")
    loglam_t = nc.dram_tensor("loglam_t", [128, SC], F32, kind="ExternalInput")
    dskip_t = nc.dram_tensor("dskip_t", [128, DC], F32, kind="ExternalInput")
    b1f_t = nc.dram_tensor("b1f_t", [128, FB], F32, kind="ExternalInput")
    b2f_t = nc.dram_tensor("b2f_t", [128, DC], F32, kind="ExternalInput")
    b1e_t = nc.dram_tensor("b1e_t", [128, EFB], F32, kind="ExternalInput")
    b2e_t = nc.dram_tensor("b2e_t", [128, DC], F32, kind="ExternalInput")
    g1c_t = nc.dram_tensor("g1c_t", [128, DC], F32, kind="ExternalInput")
    b1c_t = nc.dram_tensor("b1c_t", [128, DC], F32, kind="ExternalInput")
    g2c_t = nc.dram_tensor("g2c_t", [128, DC], F32, kind="ExternalInput")
    b2c_t = nc.dram_tensor("b2c_t", [128, DC], F32, kind="ExternalInput")
    rankv = nc.dram_tensor("rankv", [128, 1], F32, kind="ExternalInput")
    ranku = nc.dram_tensor("ranku", [1, 1], U32, kind="ExternalInput")

    out = nc.dram_tensor("out", [TPC, D], F32, kind="ExternalOutput")
    xn_slab = nc.dram_tensor("xn_slab", [TPC, D], BF16)
    xn_all = nc.dram_tensor("xn_all", [TOK, D], BF16)
    scT_d = nc.dram_tensor("scT_d", [E, TPC], F32)
    sc_all = nc.dram_tensor("sc_all", [NCORE * E, TPC], F32)
    idpub = nc.dram_tensor("idpub", [NCORE * 512], F32)
    id_all = nc.dram_tensor("id_all", [NCORE * NCORE * 512], F32)
    slab = nc.dram_tensor("slab", [SLAB_ROWS, D], BF16)
    slab_all = nc.dram_tensor("slab_all", [NCORE * SLAB_ROWS, D], BF16)
    pbuf = nc.dram_tensor("pbuf", [NCH, 128, DC, CH], BF16)

    dbg = {}
    for name, shape, dt in debug:
        dbg[name] = nc.dram_tensor(name, list(shape), dt, kind="ExternalOutput")

    cnt = {"T": 0, "V": 0, "S": 0, "G": 0, "dS": 0, "dG": 0, "C": 0}

    with ExitStack() as _stk:
        def SEM(name):
            return _stk.enter_context(nc.semaphore(name))
        sT, sV, sS, sG, dS, dG, cC = (SEM(n) for n in
                                      ("sT", "sV", "sS", "sG", "dS", "dG", "cC"))
        sems = {"T": sT, "V": sV, "S": sS, "G": sG, "dS": dS, "dG": dG, "C": cC}

        def inc(key, instr, amt=1):
            if key in ("dS", "dG"):
                amt = 16
            instr.then_inc(sems[key], amt)
            cnt[key] += amt
            return cnt[key]

        def VS(v, instr):
            instr.then_inc(sems["V"], 1)
            cnt["V"] += 1
            v.wait_ge(sems["V"], cnt["V"])

        def SS(s, instr):
            instr.then_inc(sems["S"], 1)
            cnt["S"] += 1
            s.wait_ge(sems["S"], cnt["S"])

        def EMIT(**engines):
            with nc.Block() as blk:
                for name, fn in engines.items():
                    getattr(blk, name)(fn)

        WTS = {"T": "sT", "V": "sV", "S": "sS", "G": "sG"}

        def wt(eng, *keys):
            for k in keys:
                eng.wait_ge(sems[k], cnt[k])


        def SB(stk, name, shape, dt):
            return stk.enter_context(nc.sbuf_tensor(name, shape, dt))

        def PS(stk, name, shape, dt):
            return stk.enter_context(nc.psum_tensor(name, shape, dt))
        with ExitStack() as _so:
            ones_col = SB(_so, "ones_col", [128, 1], F32)
            ones_colb = SB(_so, "ones_colb", [128, 1], BF16)
            onesp_row = SB(_so, "onesp_row", [1, 128], F32)
            negones_row = SB(_so, "negones_row", [1, 128], F32)
            ident_bf = SB(_so, "ident_bf", [128, 128], BF16)
            ident_f = SB(_so, "ident_f", [128, 128], F32)
            ones128 = SB(_so, "ones128", [128, 128], F32)
            rbr_s = SB(_so, "rbr_s", [128, E], F32)
            lam_s = SB(_so, "lam_s", [128, SC], F32)
            lam1m_s = SB(_so, "lam1m_s", [128, SC], F32)
            loglam_s = SB(_so, "loglam_s", [128, SC], F32)
            dskip_s = SB(_so, "dskip_s", [128, DC], F32)
            b1f_s = SB(_so, "b1f_s", [128, FB], F32)
            b2f_s = SB(_so, "b2f_s", [128, DC], F32)
            b1e_s = SB(_so, "b1e_s", [128, EFB], F32)
            b2e_s = SB(_so, "b2e_s", [128, DC], F32)
            g1c_s = SB(_so, "g1c_s", [128, DC], F32)
            b1c_s = SB(_so, "b1c_s", [128, DC], F32)
            g2c_s = SB(_so, "g2c_s", [128, DC], F32)
            b2c_s = SB(_so, "b2c_s", [128, DC], F32)
            rankv_s = SB(_so, "rankv_s", [128, 1], F32)
            ranku_s = SB(_so, "ranku_s", [1, 1], U32)
            carry = SB(_so, "carry", [128, SC], F32)
            def srow(i):
                return strow[0:1, i * CH:(i + 1) * CH]

            (R_SX, R_SX2, R_MU0, R_VAR0, R_W0, R_R0, R_T1, R_T2,
             R_R1, R_BX, R_TMP) = range(11)
            R_VAR1, R_W1, R_A, R_MU2 = R_SX, R_SX2, R_VAR0, R_MU0


            def _finish_early(tag):
                def fe_g(g):
                    wt(g, "T", "V", "S", "G")
                    g.wait_ge(dS, cnt["dS"])
                    g.wait_ge(dG, cnt["dG"])
                    if tag != "nocc":
                        g.wait_ge(cC, cnt["C"])
                    dmp = {None: None,
                           "xn": xn_slab.ap().rearrange("t d -> (t d)"),
                           "sc": scT_d.ap().rearrange("e t -> (e t)"),
                           "pbuf": pbuf.ap().rearrange("c p k t -> (c p k t)"),
                           "slab": slab.ap().rearrange("r d -> (r d)"),
                           "scall": sc_all.ap().rearrange("e t -> (e t)"),
                           "idall": id_all.ap()}[dump]
                    if dmp is None and dtap is None:
                        inc("dG", g.dma_start(
                            out=out.ap().rearrange("t d -> (t d)"),
                            in_=xT.ap().rearrange("d t -> (d t)")))
                    elif dmp is None:
                        pass
                    else:
                        n = min(dmp.shape[0], TPC * D)
                        inc("dG", g.dma_start(
                            out=out.ap().rearrange("t d -> (t d)")[0:n],
                            in_=dmp[0:n]))
                    g.wait_ge(dG, cnt["dG"])
                EMIT(gpsimd=fe_g)
            def p0_sync(sy):
                for dst, src in ((rbr_s, rbr), (loglam_s, loglam_t),
                                 (dskip_s, dskip_t), (b1f_s, b1f_t),
                                 (b2f_s, b2f_t), (b1e_s, b1e_t),
                                 (b2e_s, b2e_t), (g1c_s, g1c_t),
                                 (b1c_s, b1c_t), (g2c_s, g2c_t),
                                 (b2c_s, b2c_t), (rankv_s, rankv),
                                 (ranku_s, ranku)):
                    inc("dS", sy.dma_start(out=dst[:, :], in_=src[:, :]))

            def p0_vec(v):
                v.memset(ones_col[:, :], 1.0)
                v.memset(ones_colb[:, :], 1.0)
                v.memset(onesp_row[:, :], 1.0)
                v.memset(negones_row[:, :], -1.0)
                v.memset(ones128[:, :], 1.0)
                _lv = v.memset(carry[:, :], 0.0)
                inc("V", _lv)

            EMIT(sync=p0_sync, vector=p0_vec)
            d_p0 = cnt["dS"]

            def p0_g(g):
                wt(g, "V")
                inc("G", g.affine_select(
                    out=ident_f[:, :], in_=ones128[:, :],
                    pattern=[[-1, 128]], compare_op=OP.is_equal,
                    fill=0.0, base=0, channel_multiplier=1))
            EMIT(gpsimd=p0_g)

            def p0_v2(v):
                wt(v, "G")
                inc("V", v.tensor_copy(ident_bf[:, :], ident_f[:, :]))

            def p0_s(s):
                s.add_instruction(mybir.InstLoadActFuncSet(
                    name=nc.get_next_instruction_name(),
                    act_func_set_id=6, ins=[], outs=[]))
                s.wait_ge(dS, d_p0)
                inc("S", s.activation(lam_s[:, :], loglam_s[:, :], AF.Exp,
                                      scale=-1.0))
            EMIT(vector=p0_v2, scalar=p0_s)

            def p0_v3(v):
                wt(v, "S")
                VS(v, v.tensor_scalar(lam_s[:, :], lam_s[:, :], 1.0, None, OP.add))
                VS(v, v.reciprocal(lam_s[:, :], lam_s[:, :]))
                inc("V", v.tensor_scalar(lam1m_s[:, :], lam_s[:, :], -1.0, 1.0,
                                         OP.mult, OP.add))
            EMIT(vector=p0_v3)
            v_consts = cnt["V"]

            if stop_after == "p0":
                _finish_early("p0")
                nc.compile()
                return nc

            with ExitStack() as _s1:
                strow = SB(_s1, "strow", [1, CH * 11], F32)
                zrow = SB(_s1, "zrow", [128, 1024], BF16)
                w_in_b = SB(_s1, "w_in_b", [128, DC, S], BF16)
                w_out_b = SB(_s1, "w_out_b", [128, SC, D], BF16)
                w1a_b = SB(_s1, "w1a_b", [128, DC, FA], BF16)
                w2a_b = SB(_s1, "w2a_b", [128, FB, D], BF16)
                rw_b = SB(_s1, "rw_b", [128, DC, E], BF16)
                xT_c = SB(_s1, "xT_c", [128, DC, CH], F32)
                x2_c = SB(_s1, "x2_c", [128, CH], F32)
                uT_c = SB(_s1, "uT_c", [128, DC, CH], BF16)
                xnT_c = SB(_s1, "xnT_c", [128, DC, CH], BF16)
                tt_c = SB(_s1, "tt_c", [128, CH], F32)
                lam1_c = SB(_s1, "lam1_c", [128, 1], F32)
                zf_c = SB(_s1, "zf_c", [128, CH], F32)
                zb_c = SB(_s1, "zb_c", [128, SC, CH], BF16)
                a1T_c = SB(_s1, "a1T_c", [128, DC, CH], BF16)
                ds_c = SB(_s1, "ds_c", [128, CH], BF16)
                hT_c = SB(_s1, "hT_c", [128, DC, CH], BF16)
                relu_c = SB(_s1, "relu_c", [128, FB, CH], BF16)
                mt_c = SB(_s1, "mt_c", [128, CH], BF16)
                pT_db = SB(_s1, "pT_db", [128, CH], BF16)
                xnrow_c = SB(_s1, "xnrow_c", [128, D], BF16)
                lg_c = SB(_s1, "lg_c", [128, 4 * E], F32)
                sct128 = SB(_s1, "sct128", [8, 128], F32)
                rmx_c = SB(_s1, "rmx_c", [128, 4], F32)
                ps_m1 = PS(_s1, "ps_m1", [128, CH], F32)
                ps_m2 = PS(_s1, "ps_m2", [128, CH], F32)
                ps_b1 = PS(_s1, "ps_b1", [128, CH], F32)
                ps_b2 = PS(_s1, "ps_b2", [128, CH], F32)
                ps_st1 = PS(_s1, "ps_st1", [1, CH], F32)
                ps_st2 = PS(_s1, "ps_st2", [1, CH], F32)
                ps_sm = PS(_s1, "ps_sm", [128, 192], F32)
                ps_tb = PS(_s1, "ps_tb", [128, 1024], BF16)
                def z_v(v):
                    _lv = v.memset(zrow[:, :], 0.0)
                    inc("V", _lv)
                EMIT(vector=z_v)

                def w_g(g):
                    wt(g, "V")
                    if "wcast" in skip:
                        return
                    inc("dG", g.dma_start(
                        out=w_in_b[:, :, :],
                        in_=w_in_t.ap().rearrange("(c p) s -> p c s", p=128)))
                    inc("dG", g.dma_start(
                        out=w_out_b[:, :, :],
                        in_=w_out_t.ap().rearrange("(c p) d -> p c d", p=128)))
                    inc("dG", g.dma_start(
                        out=w1a_b[:, :, :],
                        in_=w1a_t.ap().rearrange("(c p) f -> p c f", p=128)))
                    inc("dG", g.dma_start(
                        out=w2a_b[:, :, :],
                        in_=w2a_t.ap().rearrange("(c p) d -> p c d", p=128)))
                    inc("dG", g.dma_start(
                        out=rw_b[:, :, :],
                        in_=rwt.ap().rearrange("(c p) e -> p c e", p=128)))
                EMIT(gpsimd=w_g)

                def zslab_g(g):
                    if "zslab" in skip:
                        return
                    wt(g, "V")
                    za = slab.ap().rearrange("(k p) d -> p k d", p=128)
                    for k in range(SLAB_ROWS // 128):
                        inc("dG", g.dma_start(
                            out=za[:, k, :], in_=zrow[:, :]))
                EMIT(gpsimd=zslab_g)
                d_weights = cnt["dG"]

                def w_s(s):
                    if "ws" in skip or "wcast" in skip:
                        return
                    s.wait_ge(dG, d_weights)
                    s.wait_ge(sV, v_consts)
                    for sc in range(SC):
                        inc("S", s.activation(
                            w_out_b[:, sc, :], w_out_b[:, sc, :], AF.Copy,
                            scale=lam1m_s[:, sc:sc + 1]))
                EMIT(scalar=w_s)

                if stop_after == "w":
                    _finish_early("w")
                    nc.compile()
                    return nc

                for c in range(nchunks):
                    def x_ld(sy, c=c):
                        wt(sy, "T", "V", "S")
                        inc("dS", sy.dma_start(
                            out=xT_c[:, :, :],
                            in_=xT.ap().rearrange("(k p) t -> p k t", p=128)
                            [:, :, c * CH:(c + 1) * CH]))
                    EMIT(sync=x_ld)
                    d_x = cnt["dS"]

                    def st_t(t):
                        t.wait_ge(dS, d_x)
                        for dc in range(DC):
                            mm = t.matmul(ps_st1[0:1, :], ones_col[:, :],
                                          xT_c[:, dc, :],
                                          start=(dc == 0), stop=(dc == DC - 1))
                        inc("T", mm)
                    EMIT(tensor=st_t)

                    for dc in range(DC):
                        def sq_s(s, dc=dc):
                            s.wait_ge(dS, d_x)
                            if dc > 0:
                                wt(s, "T")
                            inc("S", s.activation(x2_c[:, :], xT_c[:, dc, :],
                                                  AF.Square))

                        def sq_t(t, dc=dc):
                            wt(t, "S")
                            inc("T", t.matmul(
                                ps_st2[0:1, :], ones_col[:, :], x2_c[:, :],
                                start=(dc == 0), stop=(dc == DC - 1)))
                        EMIT(scalar=sq_s, tensor=sq_t)

                    def stv(v):
                        wt(v, "T")
                        v.tensor_copy(srow(R_SX), ps_st1[0:1, :])
                        v.tensor_copy(srow(R_SX2), ps_st2[0:1, :])
                        v.tensor_scalar(srow(R_MU0), srow(R_SX), 1.0 / D, None, OP.mult)
                        v.tensor_mul(srow(R_T1), srow(R_MU0), srow(R_MU0))
                        v.tensor_scalar(srow(R_T2), srow(R_SX2), 1.0 / D, None, OP.mult)
                        v.tensor_sub(srow(R_VAR0), srow(R_T2), srow(R_T1))
                        _lv = v.tensor_scalar(srow(R_W0), srow(R_VAR0), LN_EPS, None, OP.add)
                        inc("V", _lv)
                    EMIT(vector=stv)

                    if c == 0 and stop_after == "s1":
                        _finish_early(stop_after)
                        nc.compile()
                        return nc


                    def mk_rsqrt(w_i, out_i):
                        def rs_s(s):
                            wt(s, "V")
                            s.activation(srow(R_TMP), srow(w_i), AF.Ln)
                            inc("S", s.activation(srow(out_i), srow(R_TMP),
                                                  AF.Exp, scale=-0.5))
                        EMIT(scalar=rs_s)

                    mk_rsqrt(R_W0, R_R0)

                    def v_v1(v):
                        wt(v, "S")
                        v.tensor_mul(srow(R_T1), srow(R_R0), srow(R_R0))
                        v.tensor_mul(srow(R_VAR1), srow(R_VAR0), srow(R_T1))
                        _lv = v.tensor_scalar(srow(R_W1), srow(R_VAR1), LN_EPS, None, OP.add)
                        inc("V", _lv)
                    EMIT(vector=v_v1)
                    mk_rsqrt(R_W1, R_R1)

                    def v_ab(v):
                        wt(v, "S")
                        v.tensor_mul(srow(R_A), srow(R_R0), srow(R_R1))
                        _lv = v.tensor_mul(srow(R_BX), srow(R_MU0), srow(R_R0))
                        inc("V", _lv)
                    EMIT(vector=v_ab)

                    if c == 0 and stop_after == "s2":
                        _finish_early(stop_after)
                        nc.compile()
                        return nc


                    def bc_t(t):
                        wt(t, "V")
                        t.matmul(ps_b1[:, :], onesp_row[:, :], srow(R_R0),
                                 start=True, stop=True)
                        t.matmul(ps_b2[:, :], negones_row[:, :], srow(R_BX),
                                 start=True, stop=True)
                        inc("T", t.matmul(ps_m1[:, :], onesp_row[:, :],
                                          srow(R_A), start=True, stop=True))
                    EMIT(tensor=bc_t)

                    def xn_v(v):
                        wt(v, "T")
                        for dc in range(DC):
                            v.tensor_mul(tt_c[:, :], xT_c[:, dc, :], ps_b1[:, :])
                            _lv = v.tensor_add(xnT_c[:, dc, :], tt_c[:, :], ps_b2[:, :])
                        inc("V", _lv)
                    EMIT(vector=xn_v)
                    for dc in range(DC):
                        def u_v(v, dc=dc):
                            if dc > 0:
                                wt(v, "S")
                            _lv = v.tensor_mul(tt_c[:, :], xT_c[:, dc, :], ps_m1[:, :])
                            inc("V", _lv)

                        def u_s(s, dc=dc):
                            wt(s, "V")
                            inc("S", s.activation(
                                uT_c[:, dc, :], tt_c[:, :], AF.Identity,
                                scale=g1c_s[:, dc:dc + 1],
                                bias=b1c_s[:, dc:dc + 1]))
                        EMIT(vector=u_v, scalar=u_s)

                    if c == 0 and stop_after == "s3":
                        _finish_early(stop_after)
                        nc.compile()
                        return nc

                    def rt_t(t):
                        wt(t, "V", "S")
                        for tb in range(4):
                            for dc in range(DC):
                                mm = t.matmul(ps_sm[:, tb * 8:(tb + 1) * 8],
                                              xnT_c[:, dc, tb * 128:(tb + 1) * 128],
                                              rw_b[:, dc, :],
                                              start=(dc == 0), stop=(dc == DC - 1))
                        inc("T", mm)
                    EMIT(tensor=rt_t)

                    def sm_v2(v):
                        wt(v, "T")
                        for tb in range(4):
                            VS(v, v.tensor_add(lg_c[:, tb * 8:(tb + 1) * 8],
                                               ps_sm[:, tb * 8:(tb + 1) * 8],
                                               rbr_s[:, :]))
                            VS(v, v.reduce_max(rmx_c[:, tb:tb + 1],
                                               lg_c[:, tb * 8:(tb + 1) * 8],
                                               axis=AX.X))
                            _lv = v.tensor_scalar(lg_c[:, tb * 8:(tb + 1) * 8],
                                            lg_c[:, tb * 8:(tb + 1) * 8],
                                            rmx_c[:, tb:tb + 1], None, OP.subtract)
                            VS(v, _lv)
                        inc("V", v.tensor_copy(x2_c[:, 0:1], rmx_c[:, 0:1]))
                    EMIT(vector=sm_v2)

                    def sm_s(s):
                        wt(s, "V")
                        inc("S", s.activation(lg_c[:, :], lg_c[:, :], AF.Exp))
                    EMIT(scalar=sm_s)

                    def sm_v3(v):
                        wt(v, "S")
                        for tb in range(4):
                            VS(v, v.reduce_sum(rmx_c[:, tb:tb + 1],
                                               lg_c[:, tb * 8:(tb + 1) * 8],
                                               axis=AX.X))
                        VS(v, v.reciprocal(rmx_c[:, :], rmx_c[:, :]))
                        for tb in range(4):
                            _lv = v.tensor_scalar(lg_c[:, tb * 8:(tb + 1) * 8],
                                            lg_c[:, tb * 8:(tb + 1) * 8],
                                            rmx_c[:, tb:tb + 1], None, OP.mult)
                            VS(v, _lv)
                        inc("V", v.tensor_copy(x2_c[:, 0:1], rmx_c[:, 0:1]))
                    EMIT(vector=sm_v3)

                    for tb in range(4):
                        def tr_t(t, tb=tb):
                            wt(t, "V")
                            if tb > 0:
                                t.wait_ge(dS, cnt["dS"])
                            inc("T", t.transpose(ps_sm[0:8, 64:192],
                                                 lg_c[:, tb * 8:(tb + 1) * 8],
                                                 ident_f[:, :]))

                        def tr_v(v, tb=tb):
                            wt(v, "T")
                            inc("V", v.tensor_copy(sct128[:, :],
                                                   ps_sm[0:8, 64:192]))

                        def tr_d(sy, c=c, tb=tb):
                            wt(sy, "V")
                            inc("dS", sy.dma_start(
                                out=scT_d[:, c * CH + tb * 128:
                                          c * CH + (tb + 1) * 128],
                                in_=sct128[:, :]))
                        EMIT(tensor=tr_t, vector=tr_v, sync=tr_d)

                    if c == 0 and stop_after == "s4":
                        _finish_early(stop_after)
                        nc.compile()
                        return nc

                    for sb in range(SC):
                        psx = ps_m2 if sb % 2 == 0 else ps_b1
                        def up_t(t, sb=sb, psx=psx):
                            wt(t, "S")
                            if sb >= 2:
                                wt(t, "V")
                            for dc in range(DC):
                                mm = t.matmul(psx[:, :],
                                              w_in_b[:, dc, sb * 128:(sb + 1) * 128],
                                              uT_c[:, dc, :],
                                              start=(dc == 0), stop=(dc == DC - 1))
                            inc("T", mm)

                        def up_v(v, c=c, sb=sb, psx=psx):
                            wt(v, "T")
                            VS(v, v.tensor_copy(lam1_c[:, :], lam_s[:, sb:sb + 1]))
                            lam0 = bass.AP(lam1_c, 0, [[1, 128], [0, CH]])
                            if c % 4 == 0:
                                VS(v, v.tensor_tensor_scan(zf_c[:, :], lam0,
                                                           psx[:, :], 0.0,
                                                           OP.mult, OP.add))
                            else:
                                VS(v, v.tensor_tensor_scan(zf_c[:, :], lam0,
                                                           psx[:, :],
                                                           carry[:, sb:sb + 1],
                                                           OP.mult, OP.add))
                            VS(v, v.tensor_copy(carry[:, sb:sb + 1],
                                                zf_c[:, CH - 1:CH]))
                            _lv = v.tensor_copy(zb_c[:, sb, :], zf_c[:, :])
                            inc("V", _lv)
                        EMIT(tensor=up_t, vector=up_v)

                    if c == 0 and stop_after == "s5":
                        _finish_early(stop_after)
                        nc.compile()
                        return nc

                    for db in range(DC):
                        psx = ps_m2 if db % 2 == 0 else ps_b1
                        def op_t(t, db=db, psx=psx):
                            wt(t, "V")
                            if db >= 2:
                                wt(t, "S")
                            for sc_ in range(SC):
                                mm = t.matmul(psx[:, :],
                                              w_out_b[:, sc_, db * 128:(db + 1) * 128],
                                              zb_c[:, sc_, :],
                                              start=(sc_ == 0), stop=(sc_ == SC - 1))
                            inc("T", mm)

                        def op_s(s, db=db):
                            wt(s, "V")
                            inc("S", s.activation(ds_c[:, :], uT_c[:, db, :],
                                                  AF.Copy,
                                                  scale=dskip_s[:, db:db + 1]))

                        def op_v(v, db=db, psx=psx):
                            wt(v, "T", "S")
                            v.tensor_add(tt_c[:, :], psx[:, :], ds_c[:, :])
                            _lv = v.tensor_add(a1T_c[:, db, :], tt_c[:, :],
                                         xnT_c[:, db, :])
                            inc("V", _lv)
                        EMIT(tensor=op_t, scalar=op_s, vector=op_v)

                    if c == 0 and stop_after == "s6":
                        _finish_early(stop_after)
                        nc.compile()
                        return nc

                    if dtap == "lam" and c == 0:
                        def tapl(sy):
                            wt(sy, "V", "S")
                            inc("dS", sy.dma_start(
                                out=out.ap().rearrange("t d -> (t d)")[0:1024],
                                in_=lam_s[:, :]))
                            inc("dS", sy.dma_start(
                                out=out.ap().rearrange("t d -> (t d)")[1024:2048],
                                in_=lam1m_s[:, :]))
                        EMIT(sync=tapl)
                    if dtap == "st":
                        def tap_g(sy, c=c):
                            wt(sy, "V", "S")
                            inc("dS", sy.dma_start(
                                out=out.ap().rearrange("t d -> (t d)")
                                [c * 8192: c * 8192 + CH * 11],
                                in_=strow[0:1, :]))
                        EMIT(sync=tap_g)
                    elif dtap is not None and dtap not in ("st2", "lam"):
                        def tap_d(sy, c=c):
                            wt(sy, "V", "S")
                            ttile = {"u": uT_c, "zb": zb_c, "a1": a1T_c,
                                     "xn": xnT_c}[dtap]
                            inc("dS", sy.dma_start(out=pbuf[c, :, :, :],
                                                   in_=ttile[:, :, :]))
                        EMIT(sync=tap_d)

                    def l2_t(t):
                        wt(t, "V")
                        for dc in range(DC):
                            mm = t.matmul(ps_st1[0:1, :], ones_colb[:, :],
                                          a1T_c[:, dc, :],
                                          start=(dc == 0), stop=(dc == DC - 1))
                        inc("T", mm)
                    EMIT(tensor=l2_t)
                    for dc in range(DC):
                        def sq2_s(s, dc=dc):
                            if dc > 0:
                                wt(s, "T")
                            else:
                                wt(s, "V")
                            inc("S", s.activation(x2_c[:, :], a1T_c[:, dc, :],
                                                  AF.Square))

                        def sq2_t(t, dc=dc):
                            wt(t, "S")
                            inc("T", t.matmul(
                                ps_st2[0:1, :], ones_col[:, :], x2_c[:, :],
                                start=(dc == 0), stop=(dc == DC - 1)))
                        EMIT(scalar=sq2_s, tensor=sq2_t)

                    def l2_v(v):
                        wt(v, "T")
                        v.tensor_copy(srow(R_SX), ps_st1[0:1, :])
                        v.tensor_copy(srow(R_SX2), ps_st2[0:1, :])
                        v.tensor_scalar(srow(R_MU2), srow(R_SX), 1.0 / D, None, OP.mult)
                        v.tensor_mul(srow(R_T1), srow(R_MU2), srow(R_MU2))
                        v.tensor_scalar(srow(R_T2), srow(R_SX2), 1.0 / D, None, OP.mult)
                        v.tensor_sub(srow(R_VAR0), srow(R_T2), srow(R_T1))
                        _lv = v.tensor_scalar(srow(R_W0), srow(R_VAR0), LN_EPS, None, OP.add)
                        inc("V", _lv)
                    EMIT(vector=l2_v)
                    mk_rsqrt(R_W0, R_R0)

                    def l2_v2(v):
                        wt(v, "S")
                        _lv = v.tensor_mul(srow(R_BX), srow(R_MU2), srow(R_R0))
                        inc("V", _lv)
                    EMIT(vector=l2_v2)

                    def l2bc_t(t):
                        wt(t, "V")
                        t.matmul(ps_b1[:, :], onesp_row[:, :], srow(R_R0),
                                 start=True, stop=True)
                        inc("T", t.matmul(ps_b2[:, :], negones_row[:, :],
                                          srow(R_BX), start=True, stop=True))
                    EMIT(tensor=l2bc_t)

                    for dc in range(DC):
                        def h_v(v, dc=dc):
                            if dc == 0:
                                wt(v, "T")
                            else:
                                wt(v, "S")
                            v.tensor_mul(tt_c[:, :], a1T_c[:, dc, :], ps_b1[:, :])
                            _lv = v.tensor_add(tt_c[:, :], tt_c[:, :], ps_b2[:, :])
                            inc("V", _lv)

                        def h_s(s, dc=dc):
                            wt(s, "V")
                            inc("S", s.activation(
                                hT_c[:, dc, :], tt_c[:, :], AF.Identity,
                                scale=g2c_s[:, dc:dc + 1],
                                bias=b2c_s[:, dc:dc + 1]))
                        EMIT(vector=h_v, scalar=h_s)

                    if c == 0 and stop_after == "s7":
                        _finish_early(stop_after)
                        nc.compile()
                        return nc

                    for fb in range(FB):
                        psx = ps_m2 if fb % 2 == 0 else ps_b1
                        def m1_t(t, fb=fb, psx=psx):
                            wt(t, "S")
                            if fb >= 2:
                                wt(t, "S")
                            for dc in range(DC):
                                mm = t.matmul(psx[:, :],
                                              w1a_b[:, dc, fb * 128:(fb + 1) * 128],
                                              hT_c[:, dc, :],
                                              start=(dc == 0), stop=(dc == DC - 1))
                            inc("T", mm)

                        def m1_s(s, fb=fb, psx=psx):
                            wt(s, "T")
                            inc("S", s.activation(
                                relu_c[:, fb, :], psx[:, :], AF.Relu,
                                bias=b1f_s[:, fb:fb + 1]))
                        EMIT(tensor=m1_t, scalar=m1_s)

                    for db in range(DC):
                        psx = ps_m2 if db % 2 == 0 else ps_b1
                        def m2_t(t, db=db, psx=psx):
                            wt(t, "S")
                            if db >= 2:
                                wt(t, "V")
                            for fb in range(FB):
                                mm = t.matmul(psx[:, :],
                                              w2a_b[:, fb, db * 128:(db + 1) * 128],
                                              relu_c[:, fb, :],
                                              start=(fb == 0), stop=(fb == FB - 1))
                            inc("T", mm)

                        def m2_s(s, db=db, psx=psx):
                            wt(s, "T")
                            inc("S", s.activation(
                                mt_c[:, :], psx[:, :], AF.Identity,
                                bias=b2f_s[:, db:db + 1]))

                        def m2_v(v, db=db):
                            wt(v, "S")
                            if db > 0:
                                v.wait_ge(dS, cnt["dS"])
                            v.tensor_add(tt_c[:, :], xT_c[:, db, :],
                                         a1T_c[:, db, :])
                            _lv = v.tensor_add(pT_db[:, :], tt_c[:, :], mt_c[:, :])
                            inc("V", _lv)

                        def m2_d(sy, c=c, db=db):
                            wt(sy, "V")
                            if dtap in (None, "st", "st2", "lam"):
                                inc("dS", sy.dma_start(out=pbuf[c, :, db, :],
                                                       in_=pT_db[:, :]))
                        EMIT(tensor=m2_t, scalar=m2_s, vector=m2_v, sync=m2_d)

                    if c == 0 and stop_after == "s8":
                        _finish_early(stop_after)
                        nc.compile()
                        return nc

                    for tb in range(4):
                        def xr_t(t, tb=tb):
                            if tb == 0:
                                wt(t, "V")
                            else:
                                wt(t, "S")
                            for dc in range(DC):
                                mm = t.transpose(ps_tb[:, dc * 128:(dc + 1) * 128],
                                                 xnT_c[:, dc, tb * 128:(tb + 1) * 128],
                                                 ident_bf[:, :])
                            inc("T", mm)

                        def xr_s(s, tb=tb):
                            wt(s, "T")
                            inc("S", s.copy(xnrow_c[:, :], ps_tb[:, :]))

                        def xr_d(sy, c=c, tb=tb):
                            wt(sy, "S")
                            inc("dS", sy.dma_start(
                                out=xn_slab[c * CH + tb * 128:
                                            c * CH + (tb + 1) * 128, :],
                                in_=xnrow_c[:, :]))
                        EMIT(tensor=xr_t, scalar=xr_s, sync=xr_d)

                if stop_after == "p1":
                    _finish_early("p1")
                    nc.compile()
                    return nc

                def coll_g(g):
                    g.wait_ge(dS, cnt["dS"])
                    g.wait_ge(dG, cnt["dG"])
                    inc("C", g.collective_compute(
                        "AllGather", OP.bypass,
                        replica_groups=[list(range(NCORE))],
                        ins=[scT_d.ap().opt()], outs=[sc_all.ap().opt()]))
                    inc("C", g.collective_compute(
                        "AllGather", OP.bypass,
                        replica_groups=[list(range(NCORE))],
                        ins=[xn_slab.ap().opt()], outs=[xn_all.ap().opt()]))
                EMIT(gpsimd=coll_g)
                c_sc = 1
                c_xn = 2

            if stop_after == "coll":
                _finish_early("coll")
                nc.compile()
                return nc
            if stop_after == "nocc":
                _finish_early("nocc")
                nc.compile()
                return nc

            with ExitStack() as _s4:
                cg = SB(_s4, "cg", [16, GCAP_COLS], F32)
                delt = SB(_s4, "delt", [16, 512], F32)
                mexp = SB(_s4, "mexp", [16, 512], F32)
                nfg2 = SB(_s4, "nfg2", [1, 1], U32)
                lo128 = SB(_s4, "lo128", [128, 1], F32)
                cidm = SB(_s4, "cidm", [16, 128], F32)
                cid16 = SB(_s4, "cid16", [16, 128], I16)
                idxg = SB(_s4, "idxg", [128, 128], I16)
                nfg = SB(_s4, "nfg", [1, 1], U32)
                nfr = SB(_s4, "nfr", [1, E], U32)
                nfgf = SB(_s4, "nfgf", [1, 1], F32)
                nfrf = SB(_s4, "nfrf", [1, E], F32)
                nfg16 = SB(_s4, "nfg16", [16, 1], F32)
                nfr16 = SB(_s4, "nfr16", [16, E], F32)
                rid = SB(_s4, "rid", [16, E, 32], F32)
                rv = SB(_s4, "rv", [16, E, 32], F32)
                rvm = SB(_s4, "rvm", [16, E, 32], F32)
                nfr2 = SB(_s4, "nfr2", [1, E], U32)
                ridm = SB(_s4, "ridm", [16, E, 32], F32)
                jio = SB(_s4, "jio", [16, 128], F32)
                jio32 = SB(_s4, "jio32", [16, 32], F32)
                corr = SB(_s4, "corr", [1, E], F32)
                ccum = SB(_s4, "ccum", [1, 1], F32)
                corr16 = SB(_s4, "corr16", [16, E], F32)
                hh16 = SB(_s4, "hh16", [16, 128], I16)
                hhf = SB(_s4, "hhf", [16, 128], F32)
                slotf = SB(_s4, "slotf", [16, 128], F32)
                slott = SB(_s4, "slott", [16, 128], F32)
                mtmp = SB(_s4, "mtmp", [16, 128], U32)
                slot16 = SB(_s4, "slot16", [16, 128], I16)
                idxs_t = SB(_s4, "idxs", [128, 128], I16)
                ps_a = PS(_s4, "ps_a", [128, CH], F32)
                ps_b = PS(_s4, "ps_b", [128, CH], F32)
                ps_c = PS(_s4, "ps_c", [128, CH], F32)
                ps_d = PS(_s4, "ps_d", [128, CH], F32)
                ps_sm2 = PS(_s4, "ps_sm2", [128, 64], F32)
                ps_g = PS(_s4, "ps_g", [128, CH], F32)
                ps_tp2 = PS(_s4, "ps_tp", [128, 1024], BF16)
                _s4a = _s4.enter_context(ExitStack())
                sc_pm = SB(_s4a, "sc_pm", [128, 256], F32)
                sc_j = SB(_s4a, "sc_j", [16, 2048], F32)
                hi128 = SB(_s4a, "hi128", [128, 1], F32)
                mid128 = SB(_s4a, "mid128", [128, 1], F32)
                cmpt = SB(_s4a, "cmpt", [128, 256], F32)
                redp = SB(_s4a, "redp", [128, 1], F32)
                cnt1 = SB(_s4a, "cnt1", [1, 1], F32)
                cgt = SB(_s4a, "cgt", [128, 1], U32)
                iosel = SB(_s4a, "iosel", [16, 2048], F32)
                msel = SB(_s4a, "msel", [16, 2048], U32)
                sel = SB(_s4a, "sel", [16, 2048], F32)
                sval = SB(_s4a, "sval", [16, 2048], F32)
                cid = SB(_s4a, "cid", [16, GCAP_COLS], F32)
                FFN_PS = [ps_a, ps_b, ps_c, ps_d]

                def we_g(g):
                    inc("G", g.iota(iosel[:, :], [[4096, 8], [1, 256]], base=0,
                                    channel_multiplier=256,
                                    allow_small_or_imprecise_dtypes=True))
                    inc("G", g.iota(jio[:, :], [[16, 128]], base=0,
                                    channel_multiplier=1,
                                    allow_small_or_imprecise_dtypes=True))
                    inc("G", g.iota(jio32[:, :], [[16, 32]], base=0,
                                    channel_multiplier=1,
                                    allow_small_or_imprecise_dtypes=True))
                    inc("G", g.iota(mexp[:, :], [[0, 32], [1, 16]], base=0,
                                    channel_multiplier=0,
                                    allow_small_or_imprecise_dtypes=True))
                    inc("G", g.iota(delt[:, :], [[0, 512]], base=0,
                                    channel_multiplier=1,
                                    allow_small_or_imprecise_dtypes=True))
                EMIT(gpsimd=we_g)

                def delt_v(v):
                    wt(v, "G")
                    VS(v, v.tensor_tensor(msel[:, 0:512], mexp[:, :],
                                          delt[:, :], OP.is_equal))
                    _lv = v.tensor_copy(delt[:, :], msel[:, 0:512])
                    inc("V", _lv)
                EMIT(vector=delt_v)

                def scld_g(g):
                    g.wait_ge(cC, c_sc)
                    with g.register("rk_s") as rk, g.register("ro_s") as ro:
                        g.reg_load(rk, ranku_s[0:1, 0:1])
                        g.reg_mul(ro, rk, TPC)
                        for h in range(NCORE):
                            with g.register("rh_s%d" % h) as rh:
                                g.reg_add(rh, ro, h * E * TPC)
                                inc("dG", g.dma_start(
                                    out=sc_pm[16 * h:16 * (h + 1), :],
                                    in_=bass.AP(sc_all, rh, [[256, 16], [1, 256]])))
                                inc("dG", g.dma_start(
                                    out=sc_j[:, 256 * h:256 * (h + 1)],
                                    in_=bass.AP(sc_all, rh, [[256, 16], [1, 256]])))
                EMIT(gpsimd=scld_g)
                d_scld = cnt["dG"]

                if stop_after == "q1":
                    _finish_early(stop_after)
                    nc.compile()
                    return nc


                def bi_init(v):
                    v.wait_ge(dG, d_scld)
                    VS(v, v.memset(lo128[:, :], 0.0))
                    _lv = v.memset(hi128[:, :], 1.0)
                    VS(v, _lv)
                    inc("V", v.tensor_copy(cnt1[:, :], lo128[0:1, :]))
                EMIT(vector=bi_init)

                for bi in range(40):
                    def bi_v1(v, bi=bi):
                        if bi > 0:
                            wt(v, "T")
                        VS(v, v.tensor_add(mid128[:, :], lo128[:, :],
                                           hi128[:, :]))
                        VS(v, v.tensor_scalar(mid128[:, :], mid128[:, :], 0.5,
                                              None, OP.mult))
                        v.tensor_scalar(cmpt[:, :], sc_pm[:, :], mid128[:, :],
                                        None, OP.is_gt)
                        _lv = v.reduce_sum(redp[:, :], cmpt[:, :], axis=AX.X)
                        inc("V", _lv)

                    def bi_t1(t):
                        wt(t, "V")
                        inc("T", t.matmul(ps_sm2[0:1, 32:33], ones_col[:, :],
                                          redp[:, :], start=True, stop=True))

                    def bi_v2(v):
                        wt(v, "T")
                        inc("V", v.tensor_copy(cnt1[:, :], ps_sm2[0:1, 32:33]))

                    def bi_t2(t):
                        wt(t, "V")
                        inc("T", t.matmul(ps_sm2[:, 33:34], onesp_row[:, :],
                                          cnt1[:, :], start=True, stop=True))

                    def bi_v3(v):
                        wt(v, "T")
                        VS(v, v.tensor_scalar(cgt[:, :], ps_sm2[:, 33:34],
                                              float(KSEL) - 0.5, None, OP.is_gt))
                        VS(v, v.select(lo128[:, :], cgt[:, :], mid128[:, :],
                                       lo128[:, :]))
                        VS(v, v.tensor_scalar(cgt[:, :], ps_sm2[:, 33:34],
                                              float(KSEL) - 0.5, None, OP.is_le))
                        _lv = v.select(hi128[:, :], cgt[:, :], mid128[:, :],
                                 hi128[:, :])
                        inc("V", _lv)
                    EMIT(vector=bi_v1, tensor=bi_t1)
                    EMIT(vector=bi_v2, tensor=bi_t2)
                    EMIT(vector=bi_v3)

                if stop_after == "q2":
                    _finish_early(stop_after)
                    nc.compile()
                    return nc


                def sel_v(v):
                    wt(v, "G")
                    VS(v, v.tensor_scalar(msel[:, :], sc_j[:, :],
                                          lo128[0:16, 0:1], None, OP.is_gt))
                    VS(v, v.memset(sel[:, :], -1.0))
                    VS(v, v.select(sel[:, :], msel[:, :], iosel[:, :],
                                   sel[:, :]))
                    VS(v, v.memset(sval[:, :], -1.0))
                    _lv = v.select(sval[:, :], msel[:, :], sc_j[:, :], sval[:, :])
                    inc("V", _lv)
                EMIT(vector=sel_v)

                def cp_g(g):
                    wt(g, "V")
                    g.load_library(LIB_SG)
                    for h in range(NCORE):
                        g.sparse_gather(
                            rid[:, h, :], sel[:, 256 * h:256 * (h + 1)],
                            num_found=nfr[0:1, h:h + 1])
                        inc("G", g.sparse_gather(
                            rv[:, h, :], sval[:, 256 * h:256 * (h + 1)],
                            num_found=nfr2[0:1, h:h + 1]))
                EMIT(gpsimd=cp_g)

                def nf_v(v):
                    wt(v, "G")
                    _lv = v.tensor_copy(nfrf[:, :], nfr[:, :])
                    inc("V", _lv)
                EMIT(vector=nf_v)

                def nf_t(t):
                    wt(t, "V")
                    inc("T", t.matmul(ps_sm2[0:16, 8:16], onesp_row[0:1, 0:16],
                                      nfrf[:, :], start=True, stop=True))
                EMIT(tensor=nf_t)

                def nf_v2(v):
                    wt(v, "T")
                    VS(v, v.tensor_copy(nfr16[:, :], ps_sm2[0:16, 8:16]))
                    for h in range(NCORE):
                        VS(v, v.tensor_scalar(mtmp[:, 0:32], jio32[:, :],
                                              nfr16[:, h:h + 1], None, OP.is_lt))
                        VS(v, v.memset(ridm[:, h, :], -1.0))
                        VS(v, v.select(ridm[:, h, :], mtmp[:, 0:32],
                                       rid[:, h, :], ridm[:, h, :]))
                        VS(v, v.memset(rvm[:, h, :], -1.0))
                        _lv = v.select(rvm[:, h, :], mtmp[:, 0:32], rv[:, h, :],
                                 rvm[:, h, :])
                        VS(v, _lv)
                    inc("V", v.tensor_copy(mtmp[:, 0:8], nfr16[:, :]))
                EMIT(vector=nf_v2)

                def cp2_g(g):
                    wt(g, "V")
                    inc("G", g.sparse_gather(
                        cid[:, :], ridm.ap().rearrange("q h c -> q (h c)"),
                        num_found=nfg[:, :]))
                    inc("G", g.sparse_gather(
                        cg[:, :], rvm.ap().rearrange("q h c -> q (h c)"),
                        num_found=nfg2[:, :]))
                EMIT(gpsimd=cp2_g)

                def nf2_v(v):
                    wt(v, "G")
                    _lv = v.tensor_copy(nfgf[:, :], nfg[:, :])
                    inc("V", _lv)
                EMIT(vector=nf2_v)

                def nf2_t(t):
                    wt(t, "V")
                    inc("T", t.matmul(ps_sm2[0:16, 1:2], onesp_row[0:1, 0:16],
                                      nfgf[:, :], start=True, stop=True))
                EMIT(tensor=nf2_t)

                def nf2_v2(v):
                    wt(v, "T")
                    VS(v, v.tensor_copy(nfg16[:, :], ps_sm2[0:16, 1:2]))
                    VS(v, v.tensor_scalar(mtmp[:, :], jio[:, :], nfg16[:, :],
                                          None, OP.is_lt))
                    VS(v, v.memset(cidm[:, :], 0.0))
                    VS(v, v.select(cidm[:, :], mtmp[:, :], cid[:, 0:128],
                                   cidm[:, :]))
                    _lv = v.tensor_copy(cid16[:, :], cidm[:, :])
                    inc("V", _lv)
                EMIT(vector=nf2_v2)
                v_selmask = cnt["V"]

                if stop_after == "q3":
                    _finish_early(stop_after)
                    nc.compile()
                    return nc


                def idp_g(g):
                    wt(g, "V")
                    inc("dG", g.dma_start(
                        out=idpub.ap().rearrange("(h q c) -> q h c", q=16, c=32),
                        in_=ridm[:, :, :]))
                EMIT(gpsimd=idp_g)
                d_idp = cnt["dG"]

                def idp_c(g):
                    g.wait_ge(dG, d_idp)
                    inc("C", g.collective_compute(
                        "AllGather", OP.bypass,
                        replica_groups=[list(range(NCORE))],
                        ins=[idpub.ap().opt()], outs=[id_all.ap().opt()]))
                EMIT(gpsimd=idp_c)
                c_id = cnt["C"]

                if stop_after == "q4":
                    def _q4d(sy):
                        wt(sy, "V", "G", "T", "S")
                        sy.wait_ge(dG, cnt["dG"])
                        inc("dS", sy.dma_start(
                            out=out[0:16, 64:320],
                            in_=ridm.ap().rearrange("q h c -> q (h c)")))
                        inc("dS", sy.dma_start(
                            out=out[0:16, 320:576],
                            in_=rid.ap().rearrange("q h c -> q (h c)")))
                        inc("dS", sy.dma_start(
                            out=out[0:16, 576:584], in_=nfr16[:, :]))
                        inc("dS", sy.dma_start(
                            out=out[0:16, 584:840],
                            in_=rv.ap().rearrange("q h c -> q (h c)")))
                        sy.wait_ge(dS, cnt["dS"])
                    def _q4g(g):
                        wt(g, "T", "V", "S", "G")
                        g.wait_ge(dS, cnt["dS"])
                        g.wait_ge(dG, cnt["dG"])
                        g.wait_ge(cC, cnt["C"])
                    EMIT(sync=_q4d, gpsimd=_q4g)
                    nc.compile()
                    return nc


                def ig_g(g):
                    wt(g, "V")
                    for rep in range(8):
                        inc("dG", g.dma_start(out=idxg[rep * 16:(rep + 1) * 16, :],
                                              in_=cid16[:, :]))
                EMIT(gpsimd=ig_g)
                d_idxg = cnt["dG"]

                def mlplib_g(g):
                    g.wait_ge(dG, d_idxg)
                    g.load_library(LIB_MLP)
                EMIT(gpsimd=mlplib_g)

                def slot_v(v):
                    wt(v, "V")
                    VS(v, v.tensor_copy(hh16[:, :], cidm[:, :]))
                    VS(v, v.tensor_scalar(hh16[:, :], hh16[:, :], 12, None,
                                          OP.logical_shift_right))
                    VS(v, v.tensor_copy(hhf[:, :], hh16[:, :]))
                    VS(v, v.memset(ccum[:, :], 0.0))
                    for h in range(NCORE):
                        VS(v, v.tensor_scalar(corr[0:1, h:h + 1], ccum[:, :],
                                              -1.0, 512.0 * h, OP.mult, OP.add))
                        _lv = v.tensor_add(ccum[:, :], ccum[:, :],
                                           nfrf[0:1, h:h + 1])
                        VS(v, _lv)
                    inc("V", v.tensor_copy(corr16[0:1, :], corr[0:1, :]))
                EMIT(vector=slot_v)

                def slot_t(t):
                    wt(t, "V")
                    inc("T", t.matmul(ps_sm2[0:16, 16:24], onesp_row[0:1, 0:16],
                                      corr[:, :], start=True, stop=True))
                EMIT(tensor=slot_t)

                def slot_v2(v):
                    wt(v, "T")
                    VS(v, v.tensor_copy(corr16[:, :], ps_sm2[0:16, 16:24]))
                    VS(v, v.tensor_copy(slotf[:, :], jio[:, :]))
                    for h in range(NCORE):
                        VS(v, v.tensor_scalar(slott[:, :], jio[:, :],
                                              corr16[:, h:h + 1], None, OP.add))
                        VS(v, v.tensor_scalar(mtmp[:, :], hhf[:, :], float(h),
                                              None, OP.is_equal))
                        VS(v, v.select(slotf[:, :], mtmp[:, :], slott[:, :],
                                       slotf[:, :]))
                    VS(v, v.tensor_scalar(mtmp[:, :], jio[:, :], nfg16[:, :],
                                          None, OP.is_lt))
                    VS(v, v.memset(slott[:, :], 4096.0))
                    VS(v, v.select(slott[:, :], mtmp[:, :], slotf[:, :],
                                   slott[:, :]))
                    _lv = v.tensor_copy(slot16[:, :], slott[:, :])
                    inc("V", _lv)
                EMIT(vector=slot_v2)

                def slot_g(g):
                    wt(g, "V")
                    for rep in range(8):
                        inc("dG", g.dma_start(out=idxs_t[rep * 16:(rep + 1) * 16, :],
                                              in_=slot16[:, :]))
                EMIT(gpsimd=slot_g)
                d_slots = cnt["dG"]

                if stop_after == "sel":
                    _finish_early("sel")
                    nc.compile()
                    return nc

                def selfree_bar(v):
                    wt(v, "G", "T", "S")
                    _lv = v.wait_ge(dG, cnt["dG"])
                    inc("V", _lv)
                EMIT(vector=selfree_bar)
                _s4a.close()
                toks = SB(_s4, "toks", [128, DC, CH], BF16)
                h1 = SB(_s4, "h1", [128, EFB, CH], BF16)
                ctb = SB(_s4, "ctb", [128, DC, CH], BF16)
                rows_c = SB(_s4, "rows_c", [128, 1, D], BF16)
                we1_b = SB(_s4, "we1_b", [128, DC, DFF], BF16)
                we2_b = SB(_s4, "we2_b", [128, EFB, D], BF16)

                def wfn_g(g):
                    wt(g, "V")
                    inc("dG", g.dma_start(
                        out=we1_b[:, :, :],
                        in_=we1.ap().rearrange("(c p) f -> p c f", p=128)))
                    inc("dG", g.dma_start(
                        out=we2_b[:, :, :],
                        in_=we2.ap().rearrange("(c p) d -> p c d", p=128)))
                EMIT(gpsimd=wfn_g)

                def ffn_wait_g(g):
                    g.wait_ge(cC, c_xn)
                    g.wait_ge(dG, cnt["dG"])
                EMIT(gpsimd=ffn_wait_g)

                for b in range(4):
                    def tk_g(g, b=b):
                        if b > 0:
                            wt(g, "T")
                        inc("dG", g.dma_gather(
                            toks[:, :, :], xn_all[:, :],
                            idxg[:, 32 * b:32 * (b + 1)], CH, CH, D,
                            transpose=True))
                    EMIT(gpsimd=tk_g)
                    d_tok = cnt["dG"]

                    for fb in range(EFB):
                        psx = FFN_PS[fb % 4]
                        def f1_t(t, b=b, fb=fb, psx=psx):
                            if fb == 0:
                                t.wait_ge(dG, d_tok)
                                wt(t, "S")
                            if fb >= 4:
                                wt(t, "S")
                            for dc in range(DC):
                                mm = t.matmul(psx[:, :],
                                              we1_b[:, dc, fb * 128:(fb + 1) * 128],
                                              toks[:, dc, :],
                                              start=(dc == 0), stop=(dc == DC - 1))
                            inc("T", mm)

                        def f1_s(s, fb=fb, psx=psx):
                            wt(s, "T")
                            inc("S", s.activation(
                                h1[:, fb, :], psx[:, :], AF.Relu,
                                bias=b1e_s[:, fb:fb + 1]))
                        EMIT(tensor=f1_t, scalar=f1_s)

                    for db in range(DC):
                        psx = FFN_PS[db % 4]
                        def f2_t(t, db=db, psx=psx):
                            wt(t, "S")
                            for fb in range(EFB):
                                mm = t.matmul(psx[:, :],
                                              we2_b[:, fb, db * 128:(db + 1) * 128],
                                              h1[:, fb, :],
                                              start=(fb == 0), stop=(fb == EFB - 1))
                            inc("T", mm)

                        def f2_s(s, db=db, psx=psx):
                            wt(s, "T")
                            inc("S", s.activation(
                                ctb[:, db, :], psx[:, :], AF.Identity,
                                bias=b2e_s[:, db:db + 1]))
                        EMIT(tensor=f2_t, scalar=f2_s)

                    def gate_v(v, b=b):
                        wt(v, "S")
                        rep_ap = bass.AP(cg, 32 * b,
                                         [[GCAP_COLS, 16], [1, 32], [0, 16]])
                        _lv = v.tensor_mul(mexp[:, :], delt[:, :], rep_ap)
                        inc("V", _lv)

                    def gate_t(t, b=b):
                        wt(t, "V")
                        inc("T", t.matmul(ps_g[:, :], ones128[0:16, :],
                                          mexp[:, :], start=True, stop=True))
                    EMIT(vector=gate_v, tensor=gate_t)

                    def gate_v2(v, b=b):
                        wt(v, "T")
                        for db in range(DC):
                            _lv = v.tensor_mul(ctb[:, db, :], ctb[:, db, :],
                                         ps_g[:, :])
                        inc("V", _lv)
                    EMIT(vector=gate_v2)

                    for tb2 in range(4):
                        gb = b * 4 + tb2
                        def ct_t(t, tb2=tb2):
                            wt(t, "V")
                            for dc in range(DC):
                                mm = t.transpose(ps_tp2[:, dc * 128:(dc + 1) * 128],
                                                 ctb[:, dc, tb2 * 128:(tb2 + 1) * 128],
                                                 ident_bf[:, :])
                            inc("T", mm)

                        def ct_s(s, tb2=tb2):
                            wt(s, "T")
                            if tb2 > 0:
                                s.wait_ge(dG, cnt["dG"])
                            inc("S", s.copy(rows_c[:, 0, :], ps_tp2[:, :]))

                        def ct_g(g, b=b, tb2=tb2):
                            wt(g, "S")
                            inc("dG", g.dma_scatter_add(
                                slab[:, :], rows_c[:, :, :],
                                idxs_t[:, 32 * b + 8 * tb2:32 * b + 8 * (tb2 + 1)],
                                128, 128, D))
                        EMIT(tensor=ct_t, scalar=ct_s, gpsimd=ct_g)

                if stop_after == "ffn":
                    _finish_early("ffn")
                    nc.compile()
                    return nc

                def slab_c(g):
                    g.wait_ge(dG, cnt["dG"])
                    inc("C", g.collective_compute(
                        "AllGather", OP.bypass,
                        replica_groups=[list(range(NCORE))],
                        ins=[slab.ap().opt()], outs=[slab_all.ap().opt()]))
                EMIT(gpsimd=slab_c)
                c_slab = cnt["C"]
                if stop_after == "ag4":
                    def _agw(v):
                        v.wait_ge(cC, c_slab)
                    EMIT(vector=_agw)
                    _finish_early("ag4")
                    nc.compile()
                    return nc

            with ExitStack() as _s5:
                accT = SB(_s5, "accT", [128, TPC, 8], BF16)
                slabg = SB(_s5, "slabg", [128, 8, 512], BF16)
                slabt = SB(_s5, "slabt", [128, 512, 8], BF16)
                hidx = SB(_s5, "hidx", [16, 32], F32)
                hidx16 = SB(_s5, "hidx16", [16, 32], I16)
                hix = SB(_s5, "hix", [128, 32], I16)
                lid = SB(_s5, "lid", [16, 32], F32)
                lidm = SB(_s5, "lidm", [16, 32], F32)
                lid16 = SB(_s5, "lid16", [16, 32], I16)
                lix = SB(_s5, "lix", [128, 32], I16)
                mh = SB(_s5, "mh", [16, 32], U32)
                rk16 = SB(_s5, "rk16", [16, 1], F32)
                rkneg = SB(_s5, "rkneg", [16, 1], F32)
                jh32 = SB(_s5, "jh32", [16, 32], F32)
                pb_c = SB(_s5, "pb_c", [128, DC, CH], BF16)
                of_c = SB(_s5, "of_c", [128, DC, CH], F32)
                orow = SB(_s5, "orow", [128, D], F32)
                ps_o = PS(_s5, "ps_o", [128, 1024], F32)
                def h_prep_v(v):
                    wt(v, "T", "S", "G")
                    v.wait_ge(dG, cnt["dG"])
                    v.wait_ge(dS, cnt["dS"])
                    _lv = v.tensor_scalar(rk16[:, :], rankv_s[0:16, 0:1], 512.0, None,
                                    OP.mult)
                    inc("V", _lv)
                    v.wait_ge(sems["V"], cnt["V"])
                EMIT(vector=h_prep_v)

                def h_prep_g(g):
                    wt(g, "V")
                    g.load_library(LIB_STD)
                    inc("G", g.iota(jh32[:, :], [[16, 32]], base=0,
                                    channel_multiplier=1,
                                    allow_small_or_imprecise_dtypes=True))
                    g.load_library(LIB_MLP)
                EMIT(gpsimd=h_prep_g)

                def h_prep_v2(v):
                    wt(v, "G")
                    VS(v, v.tensor_scalar(hidx[:, :], jh32[:, :], rk16[:, :],
                                          None, OP.add))
                    VS(v, v.tensor_copy(hidx16[:, :], hidx[:, :]))
                    _lv = v.memset(accT[:, :, :], 0.0)
                    inc("V", _lv)
                EMIT(vector=h_prep_v2)

                def h_prep_g2(g):
                    wt(g, "V")
                    for rep in range(8):
                        inc("dG", g.dma_start(out=hix[rep * 16:(rep + 1) * 16, :],
                                              in_=hidx16[:, :]))
                EMIT(gpsimd=h_prep_g2)
                if stop_after == "p5a":
                    def _pw(v):
                        v.wait_ge(dG, cnt["dG"])
                    EMIT(vector=_pw)
                    _finish_early("p5a")
                    nc.compile()
                    return nc

                for e in range(E):
                    def id_g(g, e=e):
                        g.wait_ge(cC, c_id)
                        if e > 0:
                            wt(g, "V")
                        with g.register("rk_i%d" % e) as rk, \
                                g.register("ro_i%d" % e) as ro:
                            g.reg_load(rk, ranku_s[0:1, 0:1])
                            g.reg_mul(ro, rk, 512)
                            g.reg_add(ro, ro, e * NCORE * 512)
                            inc("dG", g.dma_start(
                                out=hidx[:, :],
                                in_=bass.AP(id_all, ro, [[32, 16], [1, 32]])))
                    EMIT(gpsimd=id_g)
                    d_ide = cnt["dG"]

                    def id_v(v, e=e):
                        v.wait_ge(dG, d_ide)
                        VS(v, v.tensor_scalar(rkneg[:, :], rankv_s[0:16, 0:1],
                                              -4096.0, None, OP.mult))
                        VS(v, v.tensor_scalar(lid[:, :], hidx[:, :],
                                              rkneg[:, :], None, OP.add))
                        VS(v, v.tensor_scalar(mh[:, :], hidx[:, :], -0.5, None,
                                              OP.is_gt))
                        VS(v, v.memset(lidm[:, :], 0.0))
                        VS(v, v.select(lidm[:, :], mh[:, :], lid[:, :],
                                       lidm[:, :]))
                        _lv = v.tensor_copy(lid16[:, :], lidm[:, :])
                        inc("V", _lv)
                    EMIT(vector=id_v)

                    def id_g2(g, e=e):
                        wt(g, "V")
                        for rep in range(8):
                            inc("dG", g.dma_start(
                                out=lix[rep * 16:(rep + 1) * 16, :],
                                in_=lid16[:, :]))
                    EMIT(gpsimd=id_g2)
                    if stop_after == "p5i" and e == 0:
                        def _iw(sy):
                            wt(sy, "V", "G")
                            sy.wait_ge(dG, cnt["dG"])
                            inc("dS", sy.dma_start(
                                out=out[0:16, 0:32], in_=hidx[:, :]))
                            inc("dS", sy.dma_start(
                                out=out[0:16, 32:64], in_=lidm[:, :]))
                            inc("dS", sy.dma_start(
                                out=out.ap().rearrange("t d -> (t d)")
                                [65536:65536 + 32768],
                                in_=id_all.ap()))
                            inc("dS", sy.dma_start(
                                out=out[0:16, 64:320],
                                in_=ridm.ap().rearrange("q h c -> q (h c)")))
                            inc("dS", sy.dma_start(
                                out=out[0:16, 320:576],
                                in_=rid.ap().rearrange("q h c -> q (h c)")))
                            inc("dS", sy.dma_start(
                                out=out[0:16, 576:584], in_=nfr16[:, :]))
                            sy.wait_ge(dS, cnt["dS"])
                        def _iw2(g):
                            wt(g, "T", "V", "S", "G")
                            g.wait_ge(dS, cnt["dS"])
                            g.wait_ge(dG, cnt["dG"])
                            g.wait_ge(cC, cnt["C"])
                        EMIT(sync=_iw, gpsimd=_iw2)
                        nc.compile()
                        return nc

                    def sg_g(g, e=e):
                        g.wait_ge(cC, c_slab)
                        g.wait_ge(dG, cnt["dG"])
                        inc("dG", g.dma_gather(
                            slabg[:, :, :],
                            slab_all[e * SLAB_ROWS:(e + 1) * SLAB_ROWS, :],
                            hix[:, :], 512, 512, D, transpose=True))
                    EMIT(gpsimd=sg_g)
                    d_sg = cnt["dG"]

                    def tr_v(v, e=e):
                        v.wait_ge(dG, d_sg)
                        inc("V", v.tensor_copy(
                            slabt[:, :, :],
                            slabg.ap().rearrange("p s j -> p j s")))
                    EMIT(vector=tr_v)

                    def sa_g(g, e=e):
                        wt(g, "V")
                        inc("G", g.scatter_add(accT[:, :, :], lix[:, :],
                                               slabt[:, :, :], 128, TPC, 8, 512))
                    EMIT(gpsimd=sa_g)
                    if stop_after == "p5e%d" % e:
                        def _ew(v):
                            wt(v, "G")
                        EMIT(vector=_ew)
                        _finish_early(stop_after)
                        nc.compile()
                        return nc

                if stop_after == "p5":
                    _finish_early("p5")
                    nc.compile()
                    return nc

                for c in range(NCH):
                    def fp_d(sy, c=c):
                        if c == 0:
                            wt(sy, "G")
                        else:
                            wt(sy, "V")
                        inc("dS", sy.dma_start(out=pb_c[:, :, :],
                                               in_=pbuf[c, :, :, :]))
                    EMIT(sync=fp_d)
                    d_pb = cnt["dS"]

                    def fo_v(v, c=c):
                        v.wait_ge(dS, d_pb)
                        wt(v, "G")
                        for dc in range(DC):
                            acc_ap = bass.AP(
                                accT, (c * CH) * 8 + dc,
                                [[TPC * 8, 128], [8, CH]])
                            _lv = v.tensor_add(of_c[:, dc, :], pb_c[:, dc, :], acc_ap)
                        inc("V", _lv)
                    EMIT(vector=fo_v)

                    for tb in range(4):
                        def ft_t(t, tb=tb):
                            wt(t, "V")
                            if tb > 0:
                                wt(t, "S")
                            for dc in range(DC):
                                mm = t.transpose(ps_o[:, dc * 128:(dc + 1) * 128],
                                                 of_c[:, dc, tb * 128:(tb + 1) * 128],
                                                 ident_f[:, :])
                            inc("T", mm)

                        def ft_s(s, tb=tb):
                            wt(s, "T")
                            inc("S", s.copy(orow[:, :], ps_o[:, :]))

                        def ft_d(sy, c=c, tb=tb):
                            wt(sy, "S")
                            inc("dS", sy.dma_start(
                                out=out[c * CH + tb * 128:
                                        c * CH + (tb + 1) * 128, :],
                                in_=orow[:, :]))
                        EMIT(tensor=ft_t, scalar=ft_s, sync=ft_d)

                if debug:
                    def dump_g(g):
                        wt(g, "T", "V", "S", "G")
                        g.wait_ge(dS, cnt["dS"])
                        g.wait_ge(dG, cnt["dG"])
                        for name, srcb in (("d_pbuf", pbuf), ("d_xn", xn_slab),
                                           ("d_sc", scT_d), ("d_slab", slab),
                                           ("d_scall", sc_all), ("d_idall", id_all)):
                            if name in dbg:
                                inc("dG", g.dma_start(
                                    out=dbg[name].ap().rearrange("... -> (...)")
                                    if False else dbg[name].ap(),
                                    in_=srcb.ap()))
                        g.wait_ge(dG, cnt["dG"])
                    EMIT(gpsimd=dump_g)

                def fin(sy):
                    sy.wait_ge(dS, cnt["dS"])
                EMIT(sync=fin)

                if dbg:
                    def dbg_g(g):
                        wt(g, "V", "S", "T", "G")
                        g.wait_ge(dS, cnt["dS"])
                    EMIT(gpsimd=dbg_g)

    nc.compile()
    return nc


_B = 16
_T = 2048


def _stage_inputs(inputs):
    f = np.float32
    x = np.asarray(inputs["L0_attn_x"], f).reshape(TOK, D)
    w_in = np.asarray(inputs["L0_attn_w_in"], f)
    w_out = np.asarray(inputs["L0_attn_w_out"], f)
    loglam = np.asarray(inputs["L0_attn_log_lambda"], f)
    dskip = np.asarray(inputs["L0_attn_D_skip"], f)
    g1 = np.asarray(inputs["L0_attn_ln1_g"], f)
    b1 = np.asarray(inputs["L0_attn_ln1_b"], f)
    g2 = np.asarray(inputs["L0_attn_ln2_g"], f)
    b2 = np.asarray(inputs["L0_attn_ln2_b"], f)
    aw1 = np.asarray(inputs["L0_attn_w1"], f)
    ab1 = np.asarray(inputs["L0_attn_b1"], f)
    aw2 = np.asarray(inputs["L0_attn_w2"], f)
    ab2 = np.asarray(inputs["L0_attn_b2"], f)
    rwm = np.asarray(inputs["L0_ffn_router_w"], f)
    rb = np.asarray(inputs["L0_ffn_router_b"], f)
    fw1 = np.asarray(inputs["L0_ffn_w1"], f)
    fb1 = np.asarray(inputs["L0_ffn_b1"], f)
    fw2 = np.asarray(inputs["L0_ffn_w2"], f)
    fb2 = np.asarray(inputs["L0_ffn_b2"], f)

    def pcol(vec, k):
        return np.ascontiguousarray(vec.reshape(k, 128).T)

    shared = {
        "w_in_t": np.ascontiguousarray(w_in.T),
        "w_out_t": np.ascontiguousarray(w_out.T),
        "w1a_t": np.ascontiguousarray(aw1.T),
        "w2a_t": np.ascontiguousarray(aw2.T),
        "rwt": np.ascontiguousarray(rwm),
        "rbr": np.ascontiguousarray(np.tile(rb[None, :], (128, 1))),
        "loglam_t": pcol(loglam, SC),
        "dskip_t": pcol(dskip, DC),
        "b1f_t": pcol(ab1, FB),
        "b2f_t": pcol(ab2, DC),
        "g1c_t": pcol(g1, DC),
        "b1c_t": pcol(b1, DC),
        "g2c_t": pcol(g2, DC),
        "b2c_t": pcol(b2, DC),
    }
    maps = []
    for c in range(NCORE):
        m = dict(shared)
        m["xT"] = np.ascontiguousarray(x[c * TPC:(c + 1) * TPC, :].T)
        m["we1"] = np.ascontiguousarray(fw1[c])
        m["we2"] = np.ascontiguousarray(fw2[c])
        m["b1e_t"] = pcol(fb1[c], EFB)
        m["b2e_t"] = pcol(fb2[c], DC)
        m["rankv"] = np.full((128, 1), c, f)
        m["ranku"] = np.full((1, 1), c, np.uint32)
        maps.append(m)
    return maps


_NC_CACHE = {}


def kernel(**inputs):
    key = "main"
    if key not in _NC_CACHE:
        _NC_CACHE[key] = build_program(debug=DEBUG_OUT)
    nc = _NC_CACHE[key]
    maps = _stage_inputs(inputs)
    res = run_bass_kernel_spmd(nc, maps, core_ids=list(range(NCORE)))
    parts = [res.results[c]["out"] for c in range(NCORE)]
    return np.concatenate(parts, 0).reshape(_B, _T, D).astype(np.float32)
